# version 06
# baseline (speedup 1.0000x reference)
import numpy as np
import ml_dtypes
import concourse.bass as bass
import concourse.mybir as mybir
from concourse.bass_utils import run_bass_kernel_spmd
from contextlib import ExitStack

F32 = mybir.dt.float32
BF16 = mybir.dt.bfloat16
AF = mybir.ActivationFunctionType
ALU = mybir.AluOpType
AX = mybir.AxisListType

C32 = float(1.5 * 2 ** 23)
C64 = float(1.5 * 2 ** 52)
H, DN, DR, DV = 16, 128, 64, 128
D, S, TQ, DFF = 2048, 2048, 1024, 8192
NT, NTQ = 16, 8
ISQDN = 1.0 / float(np.sqrt(np.float64(DN)))
ISQDR = 1.0 / float(np.sqrt(np.float64(DR)))

ENGS = ("sync", "scalar", "vector", "tensor", "gpsimd")


def build(dbg=False):
    nc = bass.Bass("TRN2", target_bir_lowering=False)
    kindI = "ExternalOutput" if dbg else "Internal"

    x_h = nc.dram_tensor("x", [S, D], F32, kind="ExternalInput").ap()
    wqT = nc.dram_tensor("wqT", [D, 3072], F32, kind="ExternalInput").ap()
    wkT = nc.dram_tensor("wkT", [D, 3072], F32, kind="ExternalInput").ap()
    wvT = nc.dram_tensor("wvT", [D, D], F32, kind="ExternalInput").ap()
    woT = nc.dram_tensor("woT", [D, D], F32, kind="ExternalInput").ap()
    wgT = nc.dram_tensor("wgT", [D, DFF], F32, kind="ExternalInput").ap()
    wuT = nc.dram_tensor("wuT", [D, DFF], F32, kind="ExternalInput").ap()
    wdT = nc.dram_tensor("wdT", [DFF, D], F32, kind="ExternalInput").ap()
    ctab = nc.dram_tensor("ctab", [S, 512], F32, kind="ExternalInput").ap()
    stabh = nc.dram_tensor("stab", [S, 512], F32, kind="ExternalInput").ap()
    identh = nc.dram_tensor("ident", [128, 128], BF16, kind="ExternalInput").ap()
    onesh = nc.dram_tensor("ones", [128, 128], F32, kind="ExternalInput").ap()
    csth = nc.dram_tensor("cst", [128, 8], F32, kind="ExternalInput").ap()
    out_h = nc.dram_tensor("out", [TQ, D], F32, kind="ExternalOutput").ap()
    qnT = nc.dram_tensor("qnT", [2048, TQ], BF16, kind=kindI).ap()
    qrT = nc.dram_tensor("qrT", [1024, TQ], BF16, kind=kindI).ap()
    knT = nc.dram_tensor("knT", [2048, S], BF16, kind=kindI).ap()
    krT = nc.dram_tensor("krT", [1024, S], BF16, kind=kindI).ap()
    vt = nc.dram_tensor("vt", [H * NT * 128, 128], BF16, kind=kindI).ap()
    xqTh = nc.dram_tensor("xqT", [D, TQ], BF16, kind=kindI).ap()
    atn = nc.dram_tensor("atn", [TQ, D], F32, kind=kindI).ap()
    hsp = nc.dram_tensor("hsp", [NTQ * 8 * 128, 1024], F32, kind=kindI).ap()
    dbg_t = {}
    if dbg:
        for nm, shp, dt in (("xnT_d", [128, 32768], BF16), ("stats_d", [128, 64], F32),
                            ("probs_d", [128, 2048], BF16), ("acc_d", [128, 16384], BF16),
                            ("wsv_d", [128, 12], F32)):
            dbg_t[nm] = nc.dram_tensor(nm, shp, dt, kind="ExternalOutput").ap()

    st = ExitStack()
    streams = {k: [] for k in ENGS}
    cnt = {}
    sems = {}
    free_ev = {}

    def sem(name):
        if name not in sems:
            sems[name] = st.enter_context(nc.semaphore(name))
            cnt[name] = 0
        return sems[name]

    def op(eng, method, *args, inc=None, n=1, **kw):
        s_h = sem(inc) if inc else None
        if inc:
            cnt[inc] += n

        def f(e, method=method, args=args, kw=kw, s_h=s_h, n=n):
            ins = getattr(e, method)(*args, **kw)
            if s_h is not None:
                ins.then_inc(s_h, n)
        streams[eng].append(f)

    def wait(eng, s, val=None):
        sem(s)
        v = cnt[s] if val is None else val

        def f(e, s_h=sems[s], v=v):
            e.wait_ge(s_h, v)
        streams[eng].append(f)

    def dma(queue, out, in_, s):
        op(queue, "dma_start", out, in_, inc=s, n=16)

    def acq(eng, key):
        for ev in free_ev.get(key, ()):
            wait(eng, ev[0], ev[1])

    def rel(key, *ss):
        free_ev[key] = [(s, cnt[s]) for s in ss]

    def barrier():
        live = [(k, v) for k, v in cnt.items() if v > 0]
        for eng in ENGS:
            for k, v in live:
                wait(eng, k, v)

    def vflush():
        wait("vector", "v")

    ident = st.enter_context(nc.sbuf_tensor("identS", [128, 128], BF16))
    ones = st.enter_context(nc.sbuf_tensor("onesS", [128, 128], F32))
    cst = st.enter_context(nc.sbuf_tensor("cstS", [128, 8], F32))
    kmv = st.enter_context(nc.sbuf_tensor("kmv", [128, 4], F32))
    kmb = st.enter_context(nc.sbuf_tensor("kmb", [128, 1], BF16))
    stats = st.enter_context(nc.sbuf_tensor("stats", [128, 64], F32))
    absums = st.enter_context(nc.sbuf_tensor("absums", [128, 8], F32))
    wsv = st.enter_context(nc.sbuf_tensor("wsv", [128, 12], F32))
    rowsums = st.enter_context(nc.sbuf_tensor("rowsums", [128, 128], F32))
    rrec = st.enter_context(nc.sbuf_tensor("rrec", [128, 1], F32))
    pp_tmps = st.enter_context(nc.sbuf_tensor("pptmps", [128, 192], F32))
    ppstk = ExitStack()
    pp_bufs = [ppstk.enter_context(nc.sbuf_tensor("ppb%d" % j, [128, 2048], F32)) for j in range(2)]
    sAP = ExitStack()
    xnT = sAP.enter_context(nc.sbuf_tensor("xnT", [128, 16 * 2048], BF16))

    def xnc(dj, t0, t1):
        return xnT[:, dj * 2048 + t0: dj * 2048 + t1]

    dma("sync", ident[:, :], identh[:, :], "ldc")
    dma("sync", ones[:, :], onesh[:, :], "ldc")
    dma("sync", cst[:, :], csth[:, :], "ldc")
    wait("vector", "ldc")
    op("vector", "tensor_scalar", kmv[:, 0:1], cst[:, 0:1], C64, C64, ALU.add, ALU.subtract, inc="v")
    vflush()
    op("vector", "tensor_scalar", kmv[:, 1:2], kmv[:, 0:1], -1.0, 1.0, ALU.mult, ALU.add, inc="v")
    vflush()
    op("vector", "tensor_scalar", kmv[:, 2:3], kmv[:, 1:2], -1.0, None, ALU.mult, inc="v")
    vflush()
    op("vector", "tensor_copy", kmb[:, 0:1], kmv[:, 1:2], inc="v")
    vflush()
    wait("tensor", "ldc")

    pa = ExitStack()
    xt = [pa.enter_context(nc.sbuf_tensor("xt%d" % j, [128, 2048], F32)) for j in range(2)]
    xnf = pa.enter_context(nc.sbuf_tensor("xnf", [128, 2048], F32))
    xnb = pa.enter_context(nc.sbuf_tensor("xnb", [128, 2048], BF16))
    sqs = pa.enter_context(nc.sbuf_tensor("sqs", [128, 2048], F32))
    tA = pa.enter_context(nc.sbuf_tensor("tA", [128, 2048], F32))
    tB = pa.enter_context(nc.sbuf_tensor("tB", [128, 2048], F32))
    xqb = pa.enter_context(nc.sbuf_tensor("xqb", [128, 2048], BF16))
    stg = [pa.enter_context(nc.sbuf_tensor("stgA%d" % j, [128, 2048], BF16)) for j in range(2)]
    ps_t1 = pa.enter_context(nc.psum_tensor("ps_t1", [128, 2048], BF16))
    ps_t2 = pa.enter_context(nc.psum_tensor("ps_t2", [128, 2048], BF16))

    for i in range(NT):
        sx = "xA%d" % (i % 2)
        xb = xt[i % 2]
        acq("sync", ("xt", i % 2))
        dma("sync", xb[:, :], x_h[i * 128:(i + 1) * 128, :], sx)
        wait("vector", sx)
        op("vector", "reduce_sum", stats[:, 16:17], xb[:, :], AX.X, inc="v")
        vflush()
        op("vector", "tensor_scalar", stats[:, i:i + 1], stats[:, 16:17], -1.0 / D, None,
           ALU.mult, inc="v")
        wait("scalar", sx)
        wait("scalar", "v")
        op("scalar", "activation", sqs[:, :], xb[:, :], AF.Square,
           accum_out=stats[:, 17:18], inc="a")
        wait("vector", "a")
        op("vector", "tensor_tensor", stats[:, 16:17], stats[:, 16:17], stats[:, 16:17],
           ALU.mult, inc="v")
        vflush()
        op("vector", "scalar_tensor_tensor", stats[:, 17:18], stats[:, 16:17], -1.0 / D,
           stats[:, 17:18], ALU.mult, ALU.add, inc="v")
        wait("scalar", "v")
        op("scalar", "activation", stats[:, 18:19], stats[:, 17:18], AF.Sqrt,
           bias=cst[:, 1:2], scale=cst[:, 2:3], inc="a")
        wait("vector", "a")
        op("vector", "reciprocal", stats[:, 19:20], stats[:, 18:19], inc="v")
        vflush()
        op("vector", "tensor_tensor", stats[:, 20:21], stats[:, i:i + 1], stats[:, 19:20],
           ALU.mult, inc="v")
        wait("scalar", "v")
        op("scalar", "activation", xnf[:, :], xb[:, :], AF.Identity,
           bias=stats[:, 20:21], scale=stats[:, 19:20], inc="a")
        rel(("xt", i % 2), "a")
        wait("vector", "a")
        op("vector", "tensor_copy", xnb[:, :], xnf[:, :], inc="v")
        acq("tensor", ("psA", 1))
        wait("tensor", "v")
        for dj in range(16):
            op("tensor", "transpose", ps_t1[:, dj * 128:(dj + 1) * 128],
               xnb[:, dj * 128:(dj + 1) * 128], ident[:, :], inc="t")
        wait("vector", "t")
        op("vector", "tensor_copy",
           xnT.rearrange("p (a f) -> p a f", a=16)[:, :, i * 128:(i + 1) * 128],
           ps_t1.rearrange("p (a f) -> p a f", a=16)[:, :, :], inc="v")
        rel(("psA", 1), "v")
        if i < NTQ:
            op("vector", "reduce_max", stats[:, 21:22], xnf[:, :], AX.X,
               apply_absolute_value=True, inc="v")
            vflush()
            op("vector", "tensor_scalar", stats[:, 22:23], stats[:, 21:22], 1e-5, None,
               ALU.max, inc="v")
            vflush()
            op("vector", "reciprocal", stats[:, 23:24], stats[:, 22:23], inc="v")
            vflush()
            op("vector", "tensor_scalar", stats[:, 24:25], stats[:, 23:24], 127.0, None,
               ALU.mult, inc="v")
            vflush()
            op("vector", "tensor_scalar", stats[:, 32 + i:33 + i], stats[:, 22:23],
               1.0 / 127.0, None, ALU.mult, inc="v")
            op("vector", "tensor_scalar", tA[:, :], xnf[:, :], stats[:, 24:25], None,
               ALU.mult, inc="v")
            op("vector", "tensor_scalar", tB[:, :], tA[:, :], C32, C32, ALU.add,
               ALU.subtract, inc="v")
            op("vector", "tensor_scalar", tA[:, :], tA[:, :], C64, C64, ALU.add,
               ALU.subtract, inc="v")
            op("vector", "scalar_tensor_tensor", tB[:, :], tB[:, :], kmv[:, 1:2], tA[:, :],
               ALU.mult, ALU.add, inc="v")
            op("vector", "tensor_scalar", xqb[:, :], tB[:, :], 127.0, -128.0, ALU.min,
               ALU.max, inc="v")
            acq("tensor", ("psA", 2))
            wait("tensor", "v")
            for dj in range(16):
                op("tensor", "transpose", ps_t2[:, dj * 128:(dj + 1) * 128],
                   xqb[:, dj * 128:(dj + 1) * 128], ident[:, :], inc="t")
            jj = i % 2
            acq("vector", ("stgA", jj))
            wait("vector", "t")
            op("vector", "tensor_copy", stg[jj].rearrange("p (a f) -> p a f", a=16),
               ps_t2.rearrange("p (a f) -> p a f", a=16), inc="v")
            rel(("psA", 2), "v")
            wait("sync", "v")
            dma("sync", xqTh[:, i * 128:(i + 1) * 128].rearrange("(a p) t -> p a t", p=128),
                stg[jj].rearrange("p (a f) -> p a f", a=16), "xqs%d" % jj)
            rel(("stgA", jj), "xqs%d" % jj)
    if dbg:
        wait("sync", "v")
        dma("sync", dbg_t["xnT_d"][:, :], xnT[:, :], "std")
        dma("sync", dbg_t["stats_d"][:, :], stats[:, :], "std")
    barrier()
    pa.close()

    pp_tiles = []
    for wi, wt, nr, ncl in ((0, wgT, D, DFF), (1, wuT, D, DFF), (2, wdT, DFF, D)):
        for r0 in range(0, nr, 128):
            for c0 in range(0, ncl, 2048):
                pp_tiles.append((wi, wt[r0:r0 + 128, c0:c0 + 2048]))
    pp_i = [0]

    def pp_step(kmax):
        for _ in range(kmax):
            i = pp_i[0]
            if i >= len(pp_tiles):
                return
            pp_i[0] = i + 1
            wi, ap = pp_tiles[i]
            j = i % 2
            s = "pp%d" % j
            acq("sync", ("ppb", j))
            dma("sync", pp_bufs[j][:, :], ap, s)
            wait("vector", s)
            op("vector", "reduce_sum", pp_tmps[:, i:i + 1], pp_bufs[j][:, :], AX.X,
               apply_absolute_value=True, inc="v")
            rel(("ppb", j), "v")

    pp = ExitStack()
    wsl = [pp.enter_context(nc.sbuf_tensor("wsl%d" % j, [128, 2048], BF16)) for j in range(2)]
    wch = [pp.enter_context(nc.sbuf_tensor("wch%d" % j, [128, 16 * 512], BF16)) for j in range(2)]
    pstg = [pp.enter_context(nc.sbuf_tensor("pstg%d" % j, [128, 1024], BF16)) for j in range(2)]
    rpf = pp.enter_context(nc.sbuf_tensor("rpf", [128, 512], F32))
    rp1 = pp.enter_context(nc.sbuf_tensor("rp1", [128, 512], F32))
    rp2 = pp.enter_context(nc.sbuf_tensor("rp2", [128, 512], F32))
    rpb = pp.enter_context(nc.sbuf_tensor("rpb", [128, 512], BF16))
    ctb = [pp.enter_context(nc.sbuf_tensor("ctb%d" % j, [128, 256], F32)) for j in range(2)]
    stb = [pp.enter_context(nc.sbuf_tensor("stb%d" % j, [128, 256], F32)) for j in range(2)]
    ps_p = [pp.enter_context(nc.psum_tensor("ps_p%d" % j, [128, 512], F32)) for j in range(4)]
    ps_r = pp.enter_context(nc.psum_tensor("ps_r", [128, 512], F32))
    ps_rt = pp.enter_context(nc.psum_tensor("ps_rt", [128, 512], BF16))
    pgrp = [0]

    ostg = [0]
    for ot in range(16):
        for which, wsrc, dest, tlen in (("q", wqT, qnT, TQ), ("k", wkT, knT, S)):
            j = (ot * 2 + (0 if which == "q" else 1)) % 2
            sw = "wP%d" % j
            acq("gpsimd", ("wsl", j))
            dma("gpsimd", wsl[j].rearrange("p (a b) -> p a b", a=16),
                wsrc[:, ot * 128:(ot + 1) * 128].rearrange("(a p) o -> p a o", p=128), sw)
            wait("tensor", sw)
            for th2 in range(tlen // 1024):
                jj = ostg[0] % 2
                ostg[0] += 1
                acq("vector", ("pstg", jj))
                for hf in range(2):
                    g = pgrp[0]
                    pgrp[0] += 1
                    pb = ps_p[g % 4]
                    acq("tensor", ("psp", g % 4))
                    for dj in range(16):
                        op("tensor", "matmul", pb[:, :], wsl[j][:, dj * 128:(dj + 1) * 128],
                           xnc(dj, th2 * 1024 + hf * 512, th2 * 1024 + hf * 512 + 512),
                           start=(dj == 0), stop=(dj == 15), inc=("t" if dj == 15 else None))
                    wait("vector", "t")
                    ssc = ISQDN if which == "q" else 1.0
                    op("vector", "tensor_scalar", pstg[jj][:, hf * 512:hf * 512 + 512],
                       pb[:, :], ssc, None, ALU.mult, inc="v")
                    rel(("psp", g % 4), "v")
                sso = "psP%d" % jj
                wait("sync", "v")
                dma("sync", dest[ot * 128:(ot + 1) * 128, th2 * 1024:th2 * 1024 + 1024],
                    pstg[jj][:, :], sso)
                rel(("pstg", jj), sso)
            rel(("wsl", j), "t")
        pp_step(2)

    for oc in range(4):
        j = oc % 2
        sw = "wV%d" % j
        acq("gpsimd", ("wch", j))
        dma("gpsimd", wch[j].rearrange("p (a b) -> p a b", a=16),
            wvT[:, oc * 512:(oc + 1) * 512].rearrange("(a p) o -> p a o", p=128), sw)
        wait("tensor", sw)
        for tt in range(NT):
            g = pgrp[0]
            pgrp[0] += 1
            pb = ps_p[g % 4]
            acq("tensor", ("psp", g % 4))
            for dj in range(16):
                op("tensor", "matmul", pb[:, :], xnc(dj, tt * 128, (tt + 1) * 128),
                   wch[j][:, dj * 512:(dj + 1) * 512],
                   start=(dj == 0), stop=(dj == 15), inc=("t" if dj == 15 else None))
            jj = ostg[0] % 2
            ostg[0] += 1
            acq("vector", ("pstg", jj))
            wait("vector", "t")
            op("vector", "tensor_copy", pstg[jj][:, 0:512], pb[:, :], inc="v")
            rel(("psp", g % 4), "v")
            sso = "psP%d" % jj
            wait("sync", "v")
            for hh in range(4):
                h = oc * 4 + hh
                dma("sync", vt[(h * NT + tt) * 128:(h * NT + tt) * 128 + 128, :],
                    pstg[jj][:, hh * 128:(hh + 1) * 128], sso)
            rel(("pstg", jj), sso)
        rel(("wch", j), "t")
        pp_step(3)

    def rope_block(wsrc, dest, ntt, qscale):
        for oc in range(2):
            j = oc % 2
            sw = "wR%d" % j
            acq("gpsimd", ("wch", j))
            dma("gpsimd", wch[j].rearrange("p (a b) -> p a b", a=16),
                wsrc[:, 2048 + oc * 512:2048 + (oc + 1) * 512].rearrange("(a p) o -> p a o", p=128), sw)
            for tt in range(ntt):
                sc = "ct%d" % (tt % 2)
                acq("sync", ("ctb", tt % 2))
                dma("sync", ctb[tt % 2][:, :], ctab[tt * 128:(tt + 1) * 128, oc * 256:(oc + 1) * 256], sc)
                dma("sync", stb[tt % 2][:, :], stabh[tt * 128:(tt + 1) * 128, oc * 256:(oc + 1) * 256], sc)
                wait("tensor", sw)
                acq("tensor", ("psr",))
                for dj in range(16):
                    op("tensor", "matmul", ps_r[:, :], xnc(dj, tt * 128, (tt + 1) * 128),
                       wch[j][:, dj * 512:(dj + 1) * 512],
                       start=(dj == 0), stop=(dj == 15), inc=("t" if dj == 15 else None))
                wait("vector", "t")
                wait("vector", sc)
                op("vector", "tensor_copy", rpf[:, :], ps_r[:, :], inc="v")
                rel(("psr",), "v")
                rel(("ctb", tt % 2), "v")
                r3 = rpf.rearrange("p (h two f) -> p h two f", h=8, two=2)
                o3 = rp1.rearrange("p (h two f) -> p h two f", h=8, two=2)
                w3 = rp2.rearrange("p (h two f) -> p h two f", h=8, two=2)
                c3 = ctb[tt % 2].rearrange("p (h f) -> p h f", h=8)
                s3 = stb[tt % 2].rearrange("p (h f) -> p h f", h=8)
                op("vector", "tensor_tensor", o3[:, :, 0, :], r3[:, :, 0, :], c3[:, :, :], ALU.mult)
                op("vector", "tensor_tensor", w3[:, :, 0, :], r3[:, :, 1, :], s3[:, :, :], ALU.mult)
                op("vector", "tensor_tensor", o3[:, :, 0, :], o3[:, :, 0, :], w3[:, :, 0, :], ALU.subtract)
                op("vector", "tensor_tensor", o3[:, :, 1, :], r3[:, :, 0, :], s3[:, :, :], ALU.mult)
                op("vector", "tensor_tensor", w3[:, :, 1, :], r3[:, :, 1, :], c3[:, :, :], ALU.mult)
                op("vector", "tensor_tensor", o3[:, :, 1, :], o3[:, :, 1, :], w3[:, :, 1, :], ALU.add)
                op("vector", "tensor_scalar", rpb[:, :], rp1[:, :], qscale, None, ALU.mult, inc="v")
                acq("tensor", ("psrt",))
                wait("tensor", "v")
                for q4 in range(4):
                    op("tensor", "transpose", ps_rt[:, q4 * 128:(q4 + 1) * 128],
                       rpb[:, q4 * 128:(q4 + 1) * 128], ident[:, :], inc="t")
                jj = ostg[0] % 2
                ostg[0] += 1
                acq("vector", ("pstg", jj))
                wait("vector", "t")
                op("vector", "tensor_copy", pstg[jj][:, 0:512], ps_rt[:, :], inc="v")
                rel(("psrt",), "v")
                sso = "psP%d" % jj
                wait("sync", "v")
                dma("sync",
                    dest[oc * 512:(oc + 1) * 512, tt * 128:(tt + 1) * 128].rearrange(
                        "(a p) t -> p a t", p=128),
                    pstg[jj][:, 0:512].rearrange("p (a t) -> p a t", a=4), sso)
                rel(("pstg", jj), sso)
            rel(("wch", j), "t")
            pp_step(2)
    rope_block(wqT, qrT, NTQ, ISQDR)
    rope_block(wkT, krT, NT, 1.0)
    barrier()
    pp.close()
    sAP.close()

    pb_ = ExitStack()
    qn_sb = [pb_.enter_context(nc.sbuf_tensor("qn%d" % j, [128, 1024], BF16)) for j in range(2)]
    qr_sb = [pb_.enter_context(nc.sbuf_tensor("qr%d" % j, [128, 1024], BF16)) for j in range(2)]
    kn_sb = [pb_.enter_context(nc.sbuf_tensor("kn%d" % j, [128, 2048], BF16)) for j in range(2)]
    kr_sb = [pb_.enter_context(nc.sbuf_tensor("kr%d" % j, [128, 2048], BF16)) for j in range(2)]
    v_sb = [pb_.enter_context(nc.sbuf_tensor("v%d" % j, [128, 2048], BF16)) for j in range(2)]
    ptm = [pb_.enter_context(nc.sbuf_tensor("ptm%d" % j, [128, 2048], BF16)) for j in range(2)]
    pT = pb_.enter_context(nc.sbuf_tensor("pT", [128, 16 * 1024], BF16))
    acc = pb_.enter_context(nc.sbuf_tensor("acc", [128, 16 * 1024], BF16))
    ps_s = pb_.enter_context(nc.psum_tensor("ps_s", [128, 2048], F32))
    ps_pt = pb_.enter_context(nc.psum_tensor("ps_pt", [128, 2048], BF16))
    ps_o = [pb_.enter_context(nc.psum_tensor("ps_o%d" % j, [128, 512], F32)) for j in range(2)]

    for h in range(H):
        j = h % 2
        sl = "hB%d" % j
        acq("sync", ("hbuf", j))
        dma("sync", qn_sb[j][:, :], qnT[h * 128:(h + 1) * 128, :], sl)
        dma("sync", qr_sb[j][0:64, :], qrT[h * 64:(h + 1) * 64, :], sl)
        dma("sync", kn_sb[j][:, :], knT[h * 128:(h + 1) * 128, :], sl)
        dma("sync", kr_sb[j][0:64, :], krT[h * 64:(h + 1) * 64, :], sl)
        for tt in range(NT):
            dma("sync", v_sb[j][:, tt * 128:(tt + 1) * 128],
                vt[(h * NT + tt) * 128:(h * NT + tt) * 128 + 128, :], sl)
        wait("tensor", sl)
        def norm_and_transpose(qt):
            wait("vector", "a")
            op("vector", "reciprocal", rrec[:, 0:1], rowsums[:, h * 8 + qt:h * 8 + qt + 1], inc="v")
            vflush()
            op("vector", "tensor_scalar", ptm[qt % 2][:, :], ptm[qt % 2][:, :], rrec[:, 0:1],
               None, ALU.mult, inc="v")
            acq("tensor", ("pspt",))
            wait("tensor", "v")
            for kt in range(16):
                op("tensor", "transpose", ps_pt[:, kt * 128:(kt + 1) * 128],
                   ptm[qt % 2][:, kt * 128:(kt + 1) * 128], ident[:, :], inc="t")
            rel(("ptm", qt % 2), "t")
            wait("scalar", "t")
            op("scalar", "activation",
               pT.rearrange("p (a t) -> p a t", a=16)[:, :, qt * 128:(qt + 1) * 128],
               ps_pt.rearrange("p (a t) -> p a t", a=16)[:, :, :], AF.Identity, inc="aT")
            rel(("pspt",), "aT")

        for qt in range(8):
            acq("tensor", ("pss",))
            for kc in range(4):
                op("tensor", "matmul", ps_s[:, kc * 512:(kc + 1) * 512],
                   qn_sb[j][:, qt * 128:(qt + 1) * 128], kn_sb[j][:, kc * 512:(kc + 1) * 512],
                   start=True, stop=False)
                op("tensor", "matmul", ps_s[:, kc * 512:(kc + 1) * 512],
                   qr_sb[j][0:64, qt * 128:(qt + 1) * 128], kr_sb[j][0:64, kc * 512:(kc + 1) * 512],
                   start=False, stop=True, inc="t")
            wait("vector", "t")
            op("vector", "reduce_max", stats[:, 16:17], ps_s[:, :], AX.X, negate=True, inc="v")
            acq("scalar", ("ptm", qt % 2))
            wait("scalar", "v")
            op("scalar", "activation", ptm[qt % 2][:, :], ps_s[:, :], AF.Exp,
               bias=stats[:, 16:17], accum_out=rowsums[:, h * 8 + qt:h * 8 + qt + 1], inc="a")
            rel(("pss",), "a")
            if qt > 0:
                norm_and_transpose(qt - 1)
        norm_and_transpose(7)
        if dbg and h == 0:
            wait("sync", "v")
            dma("sync", dbg_t["probs_d"][:, :], ptm[1][:, :], "std")
        acq("tensor", ("pso0",))
        acq("tensor", ("pso1",))
        wait("tensor", "aT")
        for kt in range(16):
            for hf in range(2):
                op("tensor", "matmul", ps_o[hf][:, :], v_sb[j][:, kt * 128:(kt + 1) * 128],
                   pT[:, kt * 1024 + hf * 512:kt * 1024 + (hf + 1) * 512],
                   start=(kt == 0), stop=(kt == 15), inc=("t" if kt == 15 else None))
        rel(("hbuf", j), "t")
        wait("vector", "t")
        for hf in range(2):
            op("vector", "tensor_copy", acc[:, h * 1024 + hf * 512:h * 1024 + (hf + 1) * 512],
               ps_o[hf][:, :], inc="v")
        rel(("pso0",), "v")
        rel(("pso1",), "v")
        pp_step(6)
    if dbg:
        wait("sync", "v")
        dma("sync", dbg_t["acc_d"][:, :], acc[:, :], "std")
    barrier()
    pb_.close()

    pw = ExitStack()
    wo_ch = [pw.enter_context(nc.sbuf_tensor("wo%d" % j, [128, 16 * 512], BF16)) for j in range(2)]
    ao_st = [pw.enter_context(nc.sbuf_tensor("ao%d" % j, [128, 512], F32)) for j in range(2)]
    ps_w = [pw.enter_context(nc.psum_tensor("ps_w%d" % j, [128, 512], F32)) for j in range(2)]
    for oc in range(4):
        j = oc % 2
        sw = "wWo%d" % j
        acq("gpsimd", ("woch", j))
        dma("gpsimd", wo_ch[j].rearrange("p (a b) -> p a b", a=16),
            woT[:, oc * 512:(oc + 1) * 512].rearrange("(a p) o -> p a o", p=128), sw)
        wait("tensor", sw)
        for tt in range(NTQ):
            acq("tensor", ("psw", tt % 2))
            for vj in range(16):
                op("tensor", "matmul", ps_w[tt % 2][:, :],
                   acc[:, vj * 1024 + tt * 128:vj * 1024 + (tt + 1) * 128],
                   wo_ch[j][:, vj * 512:(vj + 1) * 512],
                   start=(vj == 0), stop=(vj == 15), inc=("t" if vj == 15 else None))
            acq("vector", ("aost", tt % 2))
            wait("vector", "t")
            op("vector", "tensor_copy", ao_st[tt % 2][:, :], ps_w[tt % 2][:, :], inc="v")
            rel(("psw", tt % 2), "v")
            sso = "aoS%d" % (tt % 2)
            wait("sync", "v")
            dma("sync", atn[tt * 128:(tt + 1) * 128, oc * 512:(oc + 1) * 512],
                ao_st[tt % 2][:, :], sso)
            rel(("aost", tt % 2), sso)
        rel(("woch", j), "t")
        pp_step(4)
    pp_step(len(pp_tiles))
    barrier()
    pw.close()
    ppstk.close()

    pf0 = ExitStack()
    ps_bc = pf0.enter_context(nc.psum_tensor("ps_bc", [128, 8], F32))
    for wi in range(3):
        op("vector", "reduce_sum", absums[:, wi:wi + 1], pp_tmps[:, wi * 64:(wi + 1) * 64],
           AX.X, inc="v")
        vflush()
    wait("tensor", "v")
    op("tensor", "matmul", ps_bc[:, 0:3], ones[:, :], absums[:, 0:3], start=True, stop=True, inc="t")
    wait("vector", "t")
    for wi in range(3):
        op("vector", "tensor_scalar", wsv[:, 3 + wi:4 + wi], ps_bc[:, wi:wi + 1],
           1.0 / float(D * DFF), 1e-5, ALU.mult, ALU.max, inc="v")
        vflush()
        op("vector", "reciprocal", wsv[:, 9 + wi:10 + wi], wsv[:, 3 + wi:4 + wi], inc="v")
        vflush()
    if dbg:
        wait("sync", "v")
        dma("sync", dbg_t["wsv_d"][:, :], wsv[:, :], "std")
    barrier()
    pf0.close()

    pf1 = ExitStack()
    xqT_sb = pf1.enter_context(nc.sbuf_tensor("xqTs", [128, 16 * 1024], BF16))
    wstg = [pf1.enter_context(nc.sbuf_tensor("wst%d" % j, [128, 1024], F32)) for j in range(2)]
    tt1 = [pf1.enter_context(nc.sbuf_tensor("tt1%d" % j, [128, 1024], F32)) for j in range(2)]
    r32b = [pf1.enter_context(nc.sbuf_tensor("r32b%d" % j, [128, 1024], BF16)) for j in range(2)]
    r64b = pf1.enter_context(nc.sbuf_tensor("r64b", [128, 1024], BF16))
    cmbb = pf1.enter_context(nc.sbuf_tensor("cmbb", [128, 1024], BF16))
    wg_t = [pf1.enter_context(nc.sbuf_tensor("wg_t%d" % j, [128, 16 * 1024], BF16)) for j in range(2)]
    wu_t = [pf1.enter_context(nc.sbuf_tensor("wu_t%d" % j, [128, 16 * 1024], BF16)) for j in range(2)]
    silu_b = pf1.enter_context(nc.sbuf_tensor("silu_b", [128, 1024], F32))
    hpr = [pf1.enter_context(nc.sbuf_tensor("hpr%d" % j, [128, 1024], F32)) for j in range(2)]
    ps_g = [pf1.enter_context(nc.psum_tensor("ps_g%d" % j, [128, 512], F32)) for j in range(4)]
    ps_u = [pf1.enter_context(nc.psum_tensor("ps_u%d" % j, [128, 512], F32)) for j in range(4)]

    dma("sync", xqT_sb.rearrange("p (a t) -> p a t", a=16),
        xqTh.rearrange("(a p) t -> p a t", p=128), "ldf")
    op("vector", "memset", stats[:, 40:48], 0.0, inc="v")
    wait("tensor", "ldf")

    def xqc(dj, a, b):
        return xqT_sb[:, dj * 1024 + a: dj * 1024 + b]

    for c in range(8):
        par = c % 2
        acq("vector", ("wgt", par))
        for wi, wsrc, wdst, thc in ((0, wgT, wg_t[par], 0), (1, wuT, wu_t[par], 1)):
            for dj in range(16):
                m = wi * 16 + dj
                jj = m % 2
                mj = m % 2
                sw = "wF%d" % jj
                acq("sync", ("wstg", jj))
                dma("sync", wstg[jj][:, :],
                    wsrc[dj * 128:(dj + 1) * 128, c * 1024:(c + 1) * 1024], sw)
                wait("scalar", sw)
                acq("scalar", ("tt1", mj))
                op("scalar", "activation", tt1[mj][:, :], wstg[jj][:, :], AF.Identity,
                   scale=wsv[:, 9 + thc:10 + thc], bias=cst[:, 3:4], inc="a")
                rel(("wstg", jj), "a")
                acq("scalar", ("r32b", mj))
                op("scalar", "activation", r32b[mj][:, :], tt1[mj][:, :], AF.Identity,
                   bias=cst[:, 4:5], inc="a")
                rel(("tt1", mj), "a")
                wait("vector", "a")
                op("vector", "tensor_scalar", r64b[:, :], r32b[mj][:, :], C64, C64,
                   ALU.add, ALU.subtract, inc="v")
                op("vector", "scalar_tensor_tensor", cmbb[:, :], r32b[mj][:, :],
                   kmb[:, 0:1], r64b[:, :], ALU.mult, ALU.add, inc="vF")
                rel(("r32b", mj), "vF")
                op("vector", "tensor_scalar", wdst[:, dj * 1024:(dj + 1) * 1024],
                   cmbb[:, :], 1.0, -1.0, ALU.min, ALU.max, inc="vF")
        wait("tensor", "vF", cnt.get("vF", 0))
        for i in range(NTQ):
            ii = i % 2
            for hf in range(2):
                acq("tensor", ("psg", ii * 2 + hf))
                for dj in range(16):
                    op("tensor", "matmul", ps_g[ii * 2 + hf][:, :], xqc(dj, i * 128, (i + 1) * 128),
                       wg_t[par][:, dj * 1024 + hf * 512:dj * 1024 + (hf + 1) * 512],
                       start=(dj == 0), stop=(dj == 15), inc=("t" if dj == 15 else None))
                acq("tensor", ("psu", ii * 2 + hf))
                for dj in range(16):
                    op("tensor", "matmul", ps_u[ii * 2 + hf][:, :], xqc(dj, i * 128, (i + 1) * 128),
                       wu_t[par][:, dj * 1024 + hf * 512:dj * 1024 + (hf + 1) * 512],
                       start=(dj == 0), stop=(dj == 15), inc=("t" if dj == 15 else None))
            wait("vector", "t")
            op("vector", "tensor_scalar", stats[:, 16:17], stats[:, 32 + i:33 + i], wsv[:, 3:4],
               None, ALU.mult, inc="v")
            acq("scalar", ("silu",))
            wait("scalar", "v")
            wait("scalar", "t")
            for hf in range(2):
                op("scalar", "activation", silu_b[:, hf * 512:(hf + 1) * 512],
                   ps_g[ii * 2 + hf][:, :], AF.Silu, scale=stats[:, 16:17],
                   inc=("a" if hf == 1 else None))
            rel(("psg", ii * 2), "a")
            rel(("psg", ii * 2 + 1), "a")
            acq("vector", ("hpr", ii))
            wait("vector", "a")
            for hf in range(2):
                op("vector", "tensor_tensor", hpr[ii][:, hf * 512:(hf + 1) * 512],
                   silu_b[:, hf * 512:(hf + 1) * 512], ps_u[ii * 2 + hf][:, :], ALU.mult, inc="v")
            rel(("silu",), "v")
            rel(("psu", ii * 2), "v")
            rel(("psu", ii * 2 + 1), "v")
            op("vector", "reduce_max", stats[:, 17:18], hpr[ii][:, :], AX.X,
               apply_absolute_value=True, inc="v")
            vflush()
            op("vector", "tensor_tensor", stats[:, 40 + i:41 + i], stats[:, 40 + i:41 + i],
               stats[:, 17:18], ALU.max, inc="v")
            sso = "hS%d" % ii
            wait("sync", "v")
            dma("sync", hsp[(i * 8 + c) * 128:(i * 8 + c) * 128 + 128, :], hpr[ii][:, :], sso)
            rel(("hpr", ii), sso)
        rel(("wgt", par), "t")
    barrier()
    pf1.close()

    pf2 = ExitStack()
    wstg2 = [pf2.enter_context(nc.sbuf_tensor("ws2%d" % j, [128, 2048], F32)) for j in range(2)]
    tt2 = [pf2.enter_context(nc.sbuf_tensor("tt2%d" % j, [128, 2048], F32)) for j in range(2)]
    r32c = [pf2.enter_context(nc.sbuf_tensor("r32c%d" % j, [128, 2048], BF16)) for j in range(2)]
    r64c = pf2.enter_context(nc.sbuf_tensor("r64c", [128, 2048], BF16))
    cmbc = pf2.enter_context(nc.sbuf_tensor("cmbc", [128, 2048], BF16))
    wd_t = pf2.enter_context(nc.sbuf_tensor("wd_t", [128, 8 * 2048], BF16))
    hre = [pf2.enter_context(nc.sbuf_tensor("hre%d" % j, [128, 1024], F32)) for j in range(2)]
    t1 = pf2.enter_context(nc.sbuf_tensor("t1F", [128, 1024], F32))
    t2 = pf2.enter_context(nc.sbuf_tensor("t2F", [128, 1024], F32))
    hib = pf2.enter_context(nc.sbuf_tensor("hib", [128, 1024], BF16))
    hiT = pf2.enter_context(nc.sbuf_tensor("hiT", [128, 8 * 128], BF16))
    accd = pf2.enter_context(nc.sbuf_tensor("accd", [128, 8 * 2048], F32))
    xre = [pf2.enter_context(nc.sbuf_tensor("xre%d" % j, [128, 2048], F32)) for j in range(1)]
    are = [pf2.enter_context(nc.sbuf_tensor("are%d" % j, [128, 2048], F32)) for j in range(1)]
    ost = [pf2.enter_context(nc.sbuf_tensor("ost%d" % j, [128, 2048], F32)) for j in range(1)]
    ps_hT = pf2.enter_context(nc.psum_tensor("ps_hT", [128, 1024], BF16))
    ps_d = [pf2.enter_context(nc.psum_tensor("ps_d%d" % j, [128, 512], F32)) for j in range(4)]

    op("vector", "tensor_scalar", stats[:, 48:56], stats[:, 32:40], wsv[:, 4:5], None,
       ALU.mult, inc="v")
    vflush()
    op("vector", "tensor_tensor", stats[:, 40:48], stats[:, 40:48], stats[:, 48:56],
       ALU.mult, inc="v")
    vflush()
    op("vector", "tensor_scalar", stats[:, 40:48], stats[:, 40:48], 1e-5, None, ALU.max, inc="v")
    vflush()
    op("vector", "reciprocal", stats[:, 16:24], stats[:, 40:48], inc="v")
    vflush()
    op("vector", "tensor_scalar", stats[:, 16:24], stats[:, 16:24], 127.0, None, ALU.mult, inc="v")
    vflush()
    op("vector", "tensor_tensor", stats[:, 48:56], stats[:, 48:56], stats[:, 16:24],
       ALU.mult, inc="v")
    vflush()
    op("vector", "tensor_scalar", stats[:, 56:64], stats[:, 40:48], 1.0 / 127.0, None,
       ALU.mult, inc="v")
    vflush()
    op("vector", "tensor_scalar", stats[:, 56:64], stats[:, 56:64], wsv[:, 5:6], None,
       ALU.mult, inc="v")
    vflush()

    for c in range(8):
        acq("vector", ("wdt",))
        for fj in range(8):
            jj = fj % 2
            sw = "wD%d" % jj
            acq("sync", ("ws2", jj))
            dma("sync", wstg2[jj][:, :], wdT[(c * 8 + fj) * 128:(c * 8 + fj) * 128 + 128, :], sw)
            wait("scalar", sw)
            acq("scalar", ("tt2", jj))
            op("scalar", "activation", tt2[jj][:, :], wstg2[jj][:, :], AF.Identity,
               scale=wsv[:, 11:12], bias=cst[:, 3:4], inc="a")
            rel(("ws2", jj), "a")
            acq("scalar", ("r32c", jj))
            op("scalar", "activation", r32c[jj][:, :], tt2[jj][:, :], AF.Identity,
               bias=cst[:, 4:5], inc="a")
            rel(("tt2", jj), "a")
            wait("vector", "a")
            op("vector", "tensor_scalar", r64c[:, :], r32c[jj][:, :], C64, C64,
               ALU.add, ALU.subtract, inc="v")
            op("vector", "scalar_tensor_tensor", cmbc[:, :], r32c[jj][:, :],
               kmb[:, 0:1], r64c[:, :], ALU.mult, ALU.add, inc="vF2")
            rel(("r32c", jj), "vF2")
            op("vector", "tensor_scalar", wd_t[:, fj * 2048:(fj + 1) * 2048],
               cmbc[:, :], 1.0, -1.0, ALU.min, ALU.max, inc="vF2")
        wait("tensor", "vF2", cnt.get("vF2", 0))
        for i in range(NTQ):
            ii = i % 2
            sh = "hr%d" % ii
            acq("sync", ("hre", ii))
            dma("sync", hre[ii][:, :], hsp[(i * 8 + c) * 128:(i * 8 + c) * 128 + 128, :], sh)
            wait("vector", sh)
            op("vector", "tensor_scalar", t1[:, :], hre[ii][:, :], stats[:, 48 + i:49 + i],
               None, ALU.mult, inc="v")
            rel(("hre", ii), "v")
            op("vector", "tensor_scalar", t2[:, :], t1[:, :], C32, C32, ALU.add, ALU.subtract, inc="v")
            op("vector", "tensor_scalar", t1[:, :], t1[:, :], C64, C64, ALU.add, ALU.subtract, inc="v")
            op("vector", "scalar_tensor_tensor", t2[:, :], t2[:, :], kmv[:, 1:2], t1[:, :],
               ALU.mult, ALU.add, inc="v")
            op("vector", "tensor_scalar", hib[:, :], t2[:, :], 127.0, -128.0, ALU.min, ALU.max, inc="v")
            acq("tensor", ("pshT",))
            wait("tensor", "v")
            for q8 in range(8):
                op("tensor", "transpose", ps_hT[:, q8 * 128:(q8 + 1) * 128],
                   hib[:, q8 * 128:(q8 + 1) * 128], ident[:, :], inc="t")
            acq("vector", ("hiT",))
            wait("vector", "t")
            op("vector", "tensor_copy", hiT.rearrange("p (a t) -> p a t", a=8),
               ps_hT.rearrange("p (a t) -> p a t", a=8), inc="v")
            rel(("pshT",), "v")
            wait("tensor", "v")
            for oc in range(4):
                acq("tensor", ("psd", oc))
                for fj in range(8):
                    op("tensor", "matmul", ps_d[oc][:, :], hiT[:, fj * 128:(fj + 1) * 128],
                       wd_t[:, fj * 2048 + oc * 512:fj * 2048 + (oc + 1) * 512],
                       start=(fj == 0), stop=(fj == 7), inc=("t" if fj == 7 else None))
            rel(("hiT",), "t")
            wait("vector", "t")
            for oc in range(4):
                d_ap = accd[:, i * 2048 + oc * 512:i * 2048 + (oc + 1) * 512]
                if c == 0:
                    op("vector", "tensor_copy", d_ap, ps_d[oc][:, :], inc="v")
                else:
                    op("vector", "tensor_tensor", d_ap, d_ap, ps_d[oc][:, :], ALU.add, inc="v")
                rel(("psd", oc), "v")
        rel(("wdt",), "t")
    for i in range(NTQ):
        ii = 0
        so = "xo%d" % ii
        acq("sync", ("xre", ii))
        dma("sync", xre[ii][:, :], x_h[i * 128:(i + 1) * 128, :], so)
        dma("sync", are[ii][:, :], atn[i * 128:(i + 1) * 128, :], so)
        wait("vector", so)
        acq("vector", ("ost", ii))
        op("vector", "scalar_tensor_tensor", ost[ii][:, :],
           accd[:, i * 2048:(i + 1) * 2048], stats[:, 56 + i:57 + i],
           xre[ii][:, :], ALU.mult, ALU.add, inc="v")
        op("vector", "tensor_tensor", ost[ii][:, :], ost[ii][:, :], are[ii][:, :], ALU.add, inc="v")
        rel(("xre", ii), "v")
        sso = "sto%d" % ii
        wait("sync", "v")
        dma("sync", out_h[i * 128:(i + 1) * 128, :], ost[ii][:, :], sso)
        rel(("ost", ii), sso)
    wait("sync", "sto0", cnt["sto0"])
    if dbg:
        wait("sync", "std", cnt.get("std", 0))
    pf2.close()

    with nc.Block() as block:
        def runner(fns):
            def body(e):
                for f in fns:
                    f(e)
            return body
        block.sync(runner(streams["sync"]))
        block.scalar(runner(streams["scalar"]))
        block.vector(runner(streams["vector"]))
        block.tensor(runner(streams["tensor"]))
        block.gpsimd(runner(streams["gpsimd"]))
    st.close()
    return nc, dbg_t


def host_inputs(L0_attn_hidden_states, L0_attn_Wq, L0_attn_Wk, L0_attn_Wv, L0_attn_Wo,
                L0_ffn_w_gate, L0_ffn_w_up, L0_ffn_w_down):
    x = np.ascontiguousarray(np.asarray(L0_attn_hidden_states, dtype=np.float32))
    wqT_ = np.ascontiguousarray(np.asarray(L0_attn_Wq, np.float32).T)
    wkT_ = np.ascontiguousarray(np.asarray(L0_attn_Wk, np.float32).T)
    wvT_ = np.ascontiguousarray(np.asarray(L0_attn_Wv, np.float32).T)
    woT_ = np.ascontiguousarray(np.asarray(L0_attn_Wo, np.float32).T)
    wgT_ = np.ascontiguousarray(np.asarray(L0_ffn_w_gate, np.float32).T)
    wuT_ = np.ascontiguousarray(np.asarray(L0_ffn_w_up, np.float32).T)
    wdT_ = np.ascontiguousarray(np.asarray(L0_ffn_w_down, np.float32).T)
    inv_freq = (1.0 / (10000.0 ** (np.arange(0, DR, 2, dtype=np.float32) / np.float32(DR)))).astype(np.float32)
    t = np.arange(S, dtype=np.float32)
    freqs = (t[:, None] * inv_freq[None, :]).astype(np.float32)
    cos = np.cos(freqs).astype(np.float32)
    sin = np.sin(freqs).astype(np.float32)
    ctabH = np.ascontiguousarray(np.tile(cos, (1, H)))
    stabH = np.ascontiguousarray(np.tile(sin, (1, H)))
    identm = np.eye(128, dtype=ml_dtypes.bfloat16)
    onesm = np.ones((128, 128), np.float32)
    cstm = np.zeros((128, 8), np.float32)
    cstm[:, 0] = 1.0
    cstm[:, 1] = 1e-5
    cstm[:, 2] = 1.0 / np.float32(D)
    cstm[:, 3] = np.float32(C32)
    cstm[:, 4] = np.float32(-C32)
    in_maps = []
    for c in range(8):
        b, hh = c // 2, c % 2
        xb = np.ascontiguousarray(np.roll(x[b], -1024 * hh, axis=0))
        ct = np.ascontiguousarray(np.roll(ctabH, -1024 * hh, axis=0))
        sta = np.ascontiguousarray(np.roll(stabH, -1024 * hh, axis=0))
        in_maps.append(dict(x=xb, wqT=wqT_, wkT=wkT_, wvT=wvT_, woT=woT_, wgT=wgT_, wuT=wuT_,
                            wdT=wdT_, ctab=ct, stab=sta, ident=identm, ones=onesm, cst=cstm))
    return in_maps


def kernel(**inputs):
    in_maps = host_inputs(**inputs)
    nc, _ = build(dbg=False)
    res = run_bass_kernel_spmd(nc, in_maps, core_ids=list(range(8)))
    out = np.empty((4, S, D), np.float32)
    for c in range(8):
        b, hh = c // 2, c % 2
        out[b, hh * 1024:(hh + 1) * 1024, :] = res.results[c]["out"]
    return out


# version 07
# speedup vs baseline: 1.0339x; 1.0339x over previous
import numpy as np
import ml_dtypes
import concourse.bass as bass
import concourse.mybir as mybir
from concourse.bass_utils import run_bass_kernel_spmd
from contextlib import ExitStack

F32 = mybir.dt.float32
BF16 = mybir.dt.bfloat16
AF = mybir.ActivationFunctionType
ALU = mybir.AluOpType
AX = mybir.AxisListType

C32 = float(1.5 * 2 ** 23)
C64 = float(1.5 * 2 ** 52)
H, DN, DR, DV = 16, 128, 64, 128
D, S, TQ, DFF = 2048, 2048, 1024, 8192
NT, NTQ = 16, 8
ISQDN = 1.0 / float(np.sqrt(np.float64(DN)))
ISQDR = 1.0 / float(np.sqrt(np.float64(DR)))

ENGS = ("sync", "scalar", "vector", "tensor", "gpsimd")


def build(dbg=False):
    nc = bass.Bass("TRN2", target_bir_lowering=False)
    kindI = "ExternalOutput" if dbg else "Internal"

    x_h = nc.dram_tensor("x", [S, D], F32, kind="ExternalInput").ap()
    wqT = nc.dram_tensor("wqT", [D, 3072], F32, kind="ExternalInput").ap()
    wkT = nc.dram_tensor("wkT", [D, 3072], F32, kind="ExternalInput").ap()
    wvT = nc.dram_tensor("wvT", [D, D], F32, kind="ExternalInput").ap()
    woT = nc.dram_tensor("woT", [D, D], F32, kind="ExternalInput").ap()
    wgT = nc.dram_tensor("wgT", [D, DFF], F32, kind="ExternalInput").ap()
    wuT = nc.dram_tensor("wuT", [D, DFF], F32, kind="ExternalInput").ap()
    wdT = nc.dram_tensor("wdT", [DFF, D], F32, kind="ExternalInput").ap()
    ctab = nc.dram_tensor("ctab", [S, 512], F32, kind="ExternalInput").ap()
    stabh = nc.dram_tensor("stab", [S, 512], F32, kind="ExternalInput").ap()
    identh = nc.dram_tensor("ident", [128, 128], BF16, kind="ExternalInput").ap()
    onesh = nc.dram_tensor("ones", [128, 128], F32, kind="ExternalInput").ap()
    csth = nc.dram_tensor("cst", [128, 8], F32, kind="ExternalInput").ap()
    out_h = nc.dram_tensor("out", [TQ, D], F32, kind="ExternalOutput").ap()
    qnT = nc.dram_tensor("qnT", [2048, TQ], BF16, kind=kindI).ap()
    qrT = nc.dram_tensor("qrT", [1024, TQ], BF16, kind=kindI).ap()
    knT = nc.dram_tensor("knT", [2048, S], BF16, kind=kindI).ap()
    krT = nc.dram_tensor("krT", [1024, S], BF16, kind=kindI).ap()
    vt = nc.dram_tensor("vt", [H * NT * 128, 128], BF16, kind=kindI).ap()
    xqTh = nc.dram_tensor("xqT", [D, TQ], BF16, kind=kindI).ap()
    atn = nc.dram_tensor("atn", [TQ, D], F32, kind=kindI).ap()
    hsp = nc.dram_tensor("hsp", [NTQ * 8 * 128, 1024], F32, kind=kindI).ap()
    dbg_t = {}
    if dbg:
        for nm, shp, dt in (("xnT_d", [128, 32768], BF16), ("stats_d", [128, 64], F32),
                            ("probs_d", [128, 2048], BF16), ("acc_d", [128, 16384], BF16),
                            ("wsv_d", [128, 12], F32)):
            dbg_t[nm] = nc.dram_tensor(nm, shp, dt, kind="ExternalOutput").ap()

    st = ExitStack()
    streams = {k: [] for k in ENGS}
    cnt = {}
    sems = {}
    free_ev = {}

    def sem(name):
        if name not in sems:
            sems[name] = st.enter_context(nc.semaphore(name))
            cnt[name] = 0
        return sems[name]

    def op(eng, method, *args, inc=None, n=1, **kw):
        s_h = sem(inc) if inc else None
        if inc:
            cnt[inc] += n

        def f(e, method=method, args=args, kw=kw, s_h=s_h, n=n):
            ins = getattr(e, method)(*args, **kw)
            if s_h is not None:
                ins.then_inc(s_h, n)
        streams[eng].append(f)

    def wait(eng, s, val=None):
        sem(s)
        v = cnt[s] if val is None else val

        def f(e, s_h=sems[s], v=v):
            e.wait_ge(s_h, v)
        streams[eng].append(f)

    def dma(queue, out, in_, s):
        op(queue, "dma_start", out, in_, inc=s, n=16)

    def acq(eng, key):
        for ev in free_ev.get(key, ()):
            wait(eng, ev[0], ev[1])

    def rel(key, *ss):
        free_ev[key] = [(s, cnt[s]) for s in ss]

    def barrier():
        live = [(k, v) for k, v in cnt.items() if v > 0]
        for eng in ENGS:
            for k, v in live:
                wait(eng, k, v)

    def vflush():
        wait("vector", "v")

    ident = st.enter_context(nc.sbuf_tensor("identS", [128, 128], BF16))
    ones = st.enter_context(nc.sbuf_tensor("onesS", [128, 128], F32))
    cst = st.enter_context(nc.sbuf_tensor("cstS", [128, 8], F32))
    kmv = st.enter_context(nc.sbuf_tensor("kmv", [128, 4], F32))
    kmb = st.enter_context(nc.sbuf_tensor("kmb", [128, 1], BF16))
    stats = st.enter_context(nc.sbuf_tensor("stats", [128, 64], F32))
    absums = st.enter_context(nc.sbuf_tensor("absums", [128, 8], F32))
    wsv = st.enter_context(nc.sbuf_tensor("wsv", [128, 12], F32))
    rowsums = st.enter_context(nc.sbuf_tensor("rowsums", [128, 128], F32))
    rrec = st.enter_context(nc.sbuf_tensor("rrec", [128, 1], F32))
    pp_tmps = st.enter_context(nc.sbuf_tensor("pptmps", [128, 192], F32))
    ppstk = ExitStack()
    pp_bufs = [ppstk.enter_context(nc.sbuf_tensor("ppb%d" % j, [128, 2048], F32)) for j in range(2)]
    sAP = ExitStack()
    xnT = sAP.enter_context(nc.sbuf_tensor("xnT", [128, 16 * 2048], BF16))

    def xnc(dj, t0, t1):
        return xnT[:, dj * 2048 + t0: dj * 2048 + t1]

    dma("sync", ident[:, :], identh[:, :], "ldc")
    dma("sync", ones[:, :], onesh[:, :], "ldc")
    dma("sync", cst[:, :], csth[:, :], "ldc")
    wait("vector", "ldc")
    op("vector", "tensor_scalar", kmv[:, 0:1], cst[:, 0:1], C64, C64, ALU.add, ALU.subtract, inc="v")
    vflush()
    op("vector", "tensor_scalar", kmv[:, 1:2], kmv[:, 0:1], -1.0, 1.0, ALU.mult, ALU.add, inc="v")
    vflush()
    op("vector", "tensor_scalar", kmv[:, 2:3], kmv[:, 1:2], -1.0, None, ALU.mult, inc="v")
    vflush()
    op("vector", "tensor_copy", kmb[:, 0:1], kmv[:, 1:2], inc="v")
    vflush()
    wait("tensor", "ldc")

    pa = ExitStack()
    xt = [pa.enter_context(nc.sbuf_tensor("xt%d" % j, [128, 2048], F32)) for j in range(2)]
    xnf = pa.enter_context(nc.sbuf_tensor("xnf", [128, 2048], F32))
    xnb = pa.enter_context(nc.sbuf_tensor("xnb", [128, 2048], BF16))
    sqs = pa.enter_context(nc.sbuf_tensor("sqs", [128, 2048], F32))
    tA = pa.enter_context(nc.sbuf_tensor("tA", [128, 2048], F32))
    tB = pa.enter_context(nc.sbuf_tensor("tB", [128, 2048], F32))
    xqb = pa.enter_context(nc.sbuf_tensor("xqb", [128, 2048], BF16))
    stg = [pa.enter_context(nc.sbuf_tensor("stgA%d" % j, [128, 2048], BF16)) for j in range(2)]
    ps_t1 = pa.enter_context(nc.psum_tensor("ps_t1", [128, 2048], BF16))
    ps_t2 = pa.enter_context(nc.psum_tensor("ps_t2", [128, 2048], BF16))

    for i in range(NT):
        sx = "xA%d" % (i % 2)
        xb = xt[i % 2]
        acq("sync", ("xt", i % 2))
        dma("sync", xb[:, :], x_h[i * 128:(i + 1) * 128, :], sx)
        wait("vector", sx)
        op("vector", "reduce_sum", stats[:, 16:17], xb[:, :], AX.X, inc="v")
        vflush()
        op("vector", "tensor_scalar", stats[:, i:i + 1], stats[:, 16:17], -1.0 / D, None,
           ALU.mult, inc="v")
        wait("scalar", sx)
        wait("scalar", "v")
        op("scalar", "activation", sqs[:, :], xb[:, :], AF.Square,
           accum_out=stats[:, 17:18], inc="a")
        wait("vector", "a")
        op("vector", "tensor_tensor", stats[:, 16:17], stats[:, 16:17], stats[:, 16:17],
           ALU.mult, inc="v")
        vflush()
        op("vector", "scalar_tensor_tensor", stats[:, 17:18], stats[:, 16:17], -1.0 / D,
           stats[:, 17:18], ALU.mult, ALU.add, inc="v")
        wait("scalar", "v")
        op("scalar", "activation", stats[:, 18:19], stats[:, 17:18], AF.Sqrt,
           bias=cst[:, 1:2], scale=cst[:, 2:3], inc="a")
        wait("vector", "a")
        op("vector", "reciprocal", stats[:, 19:20], stats[:, 18:19], inc="v")
        vflush()
        op("vector", "tensor_tensor", stats[:, 20:21], stats[:, i:i + 1], stats[:, 19:20],
           ALU.mult, inc="v")
        wait("scalar", "v")
        op("scalar", "activation", xnf[:, :], xb[:, :], AF.Identity,
           bias=stats[:, 20:21], scale=stats[:, 19:20], inc="a")
        rel(("xt", i % 2), "a")
        wait("vector", "a")
        op("vector", "tensor_copy", xnb[:, :], xnf[:, :], inc="v")
        acq("tensor", ("psA", 1))
        wait("tensor", "v")
        for dj in range(16):
            op("tensor", "transpose", ps_t1[:, dj * 128:(dj + 1) * 128],
               xnb[:, dj * 128:(dj + 1) * 128], ident[:, :], inc="t")
        wait("vector", "t")
        op("vector", "tensor_copy",
           xnT.rearrange("p (a f) -> p a f", a=16)[:, :, i * 128:(i + 1) * 128],
           ps_t1.rearrange("p (a f) -> p a f", a=16)[:, :, :], inc="v")
        rel(("psA", 1), "v")
        if i < NTQ:
            op("vector", "reduce_max", stats[:, 21:22], xnf[:, :], AX.X,
               apply_absolute_value=True, inc="v")
            vflush()
            op("vector", "tensor_scalar", stats[:, 22:23], stats[:, 21:22], 1e-5, None,
               ALU.max, inc="v")
            vflush()
            op("vector", "reciprocal", stats[:, 23:24], stats[:, 22:23], inc="v")
            vflush()
            op("vector", "tensor_scalar", stats[:, 24:25], stats[:, 23:24], 127.0, None,
               ALU.mult, inc="v")
            vflush()
            op("vector", "tensor_scalar", stats[:, 32 + i:33 + i], stats[:, 22:23],
               1.0 / 127.0, None, ALU.mult, inc="v")
            op("vector", "tensor_scalar", tA[:, :], xnf[:, :], stats[:, 24:25], None,
               ALU.mult, inc="v")
            op("vector", "tensor_scalar", tB[:, :], tA[:, :], C32, C32, ALU.add,
               ALU.subtract, inc="v")
            op("vector", "tensor_scalar", tA[:, :], tA[:, :], C64, C64, ALU.add,
               ALU.subtract, inc="v")
            op("vector", "scalar_tensor_tensor", tB[:, :], tB[:, :], kmv[:, 1:2], tA[:, :],
               ALU.mult, ALU.add, inc="v")
            op("vector", "tensor_scalar", xqb[:, :], tB[:, :], 127.0, -128.0, ALU.min,
               ALU.max, inc="v")
            acq("tensor", ("psA", 2))
            wait("tensor", "v")
            for dj in range(16):
                op("tensor", "transpose", ps_t2[:, dj * 128:(dj + 1) * 128],
                   xqb[:, dj * 128:(dj + 1) * 128], ident[:, :], inc="t")
            jj = i % 2
            acq("vector", ("stgA", jj))
            wait("vector", "t")
            op("vector", "tensor_copy", stg[jj].rearrange("p (a f) -> p a f", a=16),
               ps_t2.rearrange("p (a f) -> p a f", a=16), inc="v")
            rel(("psA", 2), "v")
            wait("sync", "v")
            dma("sync", xqTh[:, i * 128:(i + 1) * 128].rearrange("(a p) t -> p a t", p=128),
                stg[jj].rearrange("p (a f) -> p a f", a=16), "xqs%d" % jj)
            rel(("stgA", jj), "xqs%d" % jj)
    if dbg:
        wait("sync", "v")
        dma("sync", dbg_t["xnT_d"][:, :], xnT[:, :], "std")
        dma("sync", dbg_t["stats_d"][:, :], stats[:, :], "std")
    barrier()
    pa.close()

    pp_tiles = []
    for wi, wt, nr, ncl in ((0, wgT, D, DFF), (1, wuT, D, DFF), (2, wdT, DFF, D)):
        for r0 in range(0, nr, 128):
            for c0 in range(0, ncl, 2048):
                pp_tiles.append((wi, wt[r0:r0 + 128, c0:c0 + 2048]))
    pp_i = [0]

    def pp_step(kmax):
        for _ in range(kmax):
            i = pp_i[0]
            if i >= len(pp_tiles):
                return
            pp_i[0] = i + 1
            wi, ap = pp_tiles[i]
            j = i % 2
            s = "pp%d" % j
            acq("sync", ("ppb", j))
            dma("sync", pp_bufs[j][:, :], ap, s)
            wait("vector", s)
            op("vector", "reduce_sum", pp_tmps[:, i:i + 1], pp_bufs[j][:, :], AX.X,
               apply_absolute_value=True, inc="v")
            rel(("ppb", j), "v")

    pp = ExitStack()
    wsl = [pp.enter_context(nc.sbuf_tensor("wsl%d" % j, [128, 2048], BF16)) for j in range(2)]
    wch = [pp.enter_context(nc.sbuf_tensor("wch%d" % j, [128, 16 * 512], BF16)) for j in range(2)]
    pstg = [pp.enter_context(nc.sbuf_tensor("pstg%d" % j, [128, 1024], BF16)) for j in range(2)]
    rpf = pp.enter_context(nc.sbuf_tensor("rpf", [128, 512], F32))
    rp1 = pp.enter_context(nc.sbuf_tensor("rp1", [128, 512], F32))
    rp2 = pp.enter_context(nc.sbuf_tensor("rp2", [128, 512], F32))
    rpb = pp.enter_context(nc.sbuf_tensor("rpb", [128, 512], BF16))
    ctb = [pp.enter_context(nc.sbuf_tensor("ctb%d" % j, [128, 256], F32)) for j in range(2)]
    stb = [pp.enter_context(nc.sbuf_tensor("stb%d" % j, [128, 256], F32)) for j in range(2)]
    ps_p = [pp.enter_context(nc.psum_tensor("ps_p%d" % j, [128, 512], F32)) for j in range(4)]
    ps_r = pp.enter_context(nc.psum_tensor("ps_r", [128, 512], F32))
    ps_rt = pp.enter_context(nc.psum_tensor("ps_rt", [128, 512], BF16))
    pgrp = [0]

    ostg = [0]
    for ot in range(16):
        for which, wsrc, dest, tlen in (("q", wqT, qnT, TQ), ("k", wkT, knT, S)):
            j = (ot * 2 + (0 if which == "q" else 1)) % 2
            sw = "wP%d" % j
            acq("gpsimd", ("wsl", j))
            dma("gpsimd", wsl[j].rearrange("p (a b) -> p a b", a=16),
                wsrc[:, ot * 128:(ot + 1) * 128].rearrange("(a p) o -> p a o", p=128), sw)
            wait("tensor", sw)
            for th2 in range(tlen // 1024):
                jj = ostg[0] % 2
                ostg[0] += 1
                acq("vector", ("pstg", jj))
                for hf in range(2):
                    g = pgrp[0]
                    pgrp[0] += 1
                    pb = ps_p[g % 4]
                    acq("tensor", ("psp", g % 4))
                    for dj in range(16):
                        op("tensor", "matmul", pb[:, :], wsl[j][:, dj * 128:(dj + 1) * 128],
                           xnc(dj, th2 * 1024 + hf * 512, th2 * 1024 + hf * 512 + 512),
                           start=(dj == 0), stop=(dj == 15), inc=("t" if dj == 15 else None))
                    wait("vector", "t")
                    ssc = ISQDN if which == "q" else 1.0
                    op("vector", "tensor_scalar", pstg[jj][:, hf * 512:hf * 512 + 512],
                       pb[:, :], ssc, None, ALU.mult, inc="v")
                    rel(("psp", g % 4), "v")
                sso = "psP%d" % jj
                wait("sync", "v")
                dma("sync", dest[ot * 128:(ot + 1) * 128, th2 * 1024:th2 * 1024 + 1024],
                    pstg[jj][:, :], sso)
                rel(("pstg", jj), sso)
            rel(("wsl", j), "t")
        pp_step(2)

    for oc in range(4):
        j = oc % 2
        sw = "wV%d" % j
        acq("gpsimd", ("wch", j))
        dma("gpsimd", wch[j].rearrange("p (a b) -> p a b", a=16),
            wvT[:, oc * 512:(oc + 1) * 512].rearrange("(a p) o -> p a o", p=128), sw)
        wait("tensor", sw)
        for tt in range(NT):
            g = pgrp[0]
            pgrp[0] += 1
            pb = ps_p[g % 4]
            acq("tensor", ("psp", g % 4))
            for dj in range(16):
                op("tensor", "matmul", pb[:, :], xnc(dj, tt * 128, (tt + 1) * 128),
                   wch[j][:, dj * 512:(dj + 1) * 512],
                   start=(dj == 0), stop=(dj == 15), inc=("t" if dj == 15 else None))
            jj = ostg[0] % 2
            ostg[0] += 1
            acq("vector", ("pstg", jj))
            wait("vector", "t")
            op("vector", "tensor_copy", pstg[jj][:, 0:512], pb[:, :], inc="v")
            rel(("psp", g % 4), "v")
            sso = "psP%d" % jj
            wait("sync", "v")
            for hh in range(4):
                h = oc * 4 + hh
                dma("sync", vt[(h * NT + tt) * 128:(h * NT + tt) * 128 + 128, :],
                    pstg[jj][:, hh * 128:(hh + 1) * 128], sso)
            rel(("pstg", jj), sso)
        rel(("wch", j), "t")
        pp_step(3)

    def rope_block(wsrc, dest, ntt, qscale):
        for oc in range(2):
            j = oc % 2
            sw = "wR%d" % j
            acq("gpsimd", ("wch", j))
            dma("gpsimd", wch[j].rearrange("p (a b) -> p a b", a=16),
                wsrc[:, 2048 + oc * 512:2048 + (oc + 1) * 512].rearrange("(a p) o -> p a o", p=128), sw)
            for tt in range(ntt):
                sc = "ct%d" % (tt % 2)
                acq("sync", ("ctb", tt % 2))
                dma("sync", ctb[tt % 2][:, :], ctab[tt * 128:(tt + 1) * 128, oc * 256:(oc + 1) * 256], sc)
                dma("sync", stb[tt % 2][:, :], stabh[tt * 128:(tt + 1) * 128, oc * 256:(oc + 1) * 256], sc)
                wait("tensor", sw)
                acq("tensor", ("psr",))
                for dj in range(16):
                    op("tensor", "matmul", ps_r[:, :], xnc(dj, tt * 128, (tt + 1) * 128),
                       wch[j][:, dj * 512:(dj + 1) * 512],
                       start=(dj == 0), stop=(dj == 15), inc=("t" if dj == 15 else None))
                wait("vector", "t")
                wait("vector", sc)
                op("vector", "tensor_copy", rpf[:, :], ps_r[:, :], inc="v")
                rel(("psr",), "v")
                rel(("ctb", tt % 2), "v")
                r3 = rpf.rearrange("p (h two f) -> p h two f", h=8, two=2)
                o3 = rp1.rearrange("p (h two f) -> p h two f", h=8, two=2)
                w3 = rp2.rearrange("p (h two f) -> p h two f", h=8, two=2)
                c3 = ctb[tt % 2].rearrange("p (h f) -> p h f", h=8)
                s3 = stb[tt % 2].rearrange("p (h f) -> p h f", h=8)
                op("vector", "tensor_tensor", o3[:, :, 0, :], r3[:, :, 0, :], c3[:, :, :], ALU.mult)
                op("vector", "tensor_tensor", w3[:, :, 0, :], r3[:, :, 1, :], s3[:, :, :], ALU.mult)
                op("vector", "tensor_tensor", o3[:, :, 0, :], o3[:, :, 0, :], w3[:, :, 0, :], ALU.subtract)
                op("vector", "tensor_tensor", o3[:, :, 1, :], r3[:, :, 0, :], s3[:, :, :], ALU.mult)
                op("vector", "tensor_tensor", w3[:, :, 1, :], r3[:, :, 1, :], c3[:, :, :], ALU.mult)
                op("vector", "tensor_tensor", o3[:, :, 1, :], o3[:, :, 1, :], w3[:, :, 1, :], ALU.add)
                op("vector", "tensor_scalar", rpb[:, :], rp1[:, :], qscale, None, ALU.mult, inc="v")
                acq("tensor", ("psrt",))
                wait("tensor", "v")
                for q4 in range(4):
                    op("tensor", "transpose", ps_rt[:, q4 * 128:(q4 + 1) * 128],
                       rpb[:, q4 * 128:(q4 + 1) * 128], ident[:, :], inc="t")
                jj = ostg[0] % 2
                ostg[0] += 1
                acq("vector", ("pstg", jj))
                wait("vector", "t")
                op("vector", "tensor_copy", pstg[jj][:, 0:512], ps_rt[:, :], inc="v")
                rel(("psrt",), "v")
                sso = "psP%d" % jj
                wait("sync", "v")
                dma("sync",
                    dest[oc * 512:(oc + 1) * 512, tt * 128:(tt + 1) * 128].rearrange(
                        "(a p) t -> p a t", p=128),
                    pstg[jj][:, 0:512].rearrange("p (a t) -> p a t", a=4), sso)
                rel(("pstg", jj), sso)
            rel(("wch", j), "t")
            pp_step(2)
    rope_block(wqT, qrT, NTQ, ISQDR)
    rope_block(wkT, krT, NT, 1.0)
    barrier()
    pp.close()
    sAP.close()

    pb_ = ExitStack()
    qn_sb = [pb_.enter_context(nc.sbuf_tensor("qn%d" % j, [128, 1024], BF16)) for j in range(2)]
    qr_sb = [pb_.enter_context(nc.sbuf_tensor("qr%d" % j, [128, 1024], BF16)) for j in range(2)]
    kn_sb = [pb_.enter_context(nc.sbuf_tensor("kn%d" % j, [128, 2048], BF16)) for j in range(2)]
    kr_sb = [pb_.enter_context(nc.sbuf_tensor("kr%d" % j, [128, 2048], BF16)) for j in range(2)]
    v_sb = [pb_.enter_context(nc.sbuf_tensor("v%d" % j, [128, 2048], BF16)) for j in range(2)]
    ptm = [pb_.enter_context(nc.sbuf_tensor("ptm%d" % j, [128, 2048], BF16)) for j in range(2)]
    pT = pb_.enter_context(nc.sbuf_tensor("pT", [128, 16 * 1024], BF16))
    acc = pb_.enter_context(nc.sbuf_tensor("acc", [128, 16 * 1024], BF16))
    ps_s = pb_.enter_context(nc.psum_tensor("ps_s", [128, 2048], F32))
    ps_pt = pb_.enter_context(nc.psum_tensor("ps_pt", [128, 2048], BF16))
    ps_o = [pb_.enter_context(nc.psum_tensor("ps_o%d" % j, [128, 512], F32)) for j in range(2)]

    for h in range(H):
        j = h % 2
        sl = "hB%d" % j
        acq("sync", ("hbuf", j))
        dma("sync", qn_sb[j][:, :], qnT[h * 128:(h + 1) * 128, :], sl)
        dma("sync", qr_sb[j][0:64, :], qrT[h * 64:(h + 1) * 64, :], sl)
        dma("sync", kn_sb[j][:, :], knT[h * 128:(h + 1) * 128, :], sl)
        dma("sync", kr_sb[j][0:64, :], krT[h * 64:(h + 1) * 64, :], sl)
        for tt in range(NT):
            dma("sync", v_sb[j][:, tt * 128:(tt + 1) * 128],
                vt[(h * NT + tt) * 128:(h * NT + tt) * 128 + 128, :], sl)
        wait("tensor", sl)
        def norm_and_transpose(qt):
            wait("vector", "a")
            op("vector", "reciprocal", rrec[:, 0:1], rowsums[:, h * 8 + qt:h * 8 + qt + 1], inc="v")
            vflush()
            op("vector", "tensor_scalar", ptm[qt % 2][:, :], ptm[qt % 2][:, :], rrec[:, 0:1],
               None, ALU.mult, inc="v")
            acq("tensor", ("pspt",))
            wait("tensor", "v")
            for kt in range(16):
                op("tensor", "transpose", ps_pt[:, kt * 128:(kt + 1) * 128],
                   ptm[qt % 2][:, kt * 128:(kt + 1) * 128], ident[:, :], inc="t")
            rel(("ptm", qt % 2), "t")
            wait("scalar", "t")
            op("scalar", "activation",
               pT.rearrange("p (a t) -> p a t", a=16)[:, :, qt * 128:(qt + 1) * 128],
               ps_pt.rearrange("p (a t) -> p a t", a=16)[:, :, :], AF.Identity, inc="aT")
            rel(("pspt",), "aT")

        for qt in range(8):
            acq("tensor", ("pss",))
            for kc in range(4):
                op("tensor", "matmul", ps_s[:, kc * 512:(kc + 1) * 512],
                   qn_sb[j][:, qt * 128:(qt + 1) * 128], kn_sb[j][:, kc * 512:(kc + 1) * 512],
                   start=True, stop=False)
                op("tensor", "matmul", ps_s[:, kc * 512:(kc + 1) * 512],
                   qr_sb[j][0:64, qt * 128:(qt + 1) * 128], kr_sb[j][0:64, kc * 512:(kc + 1) * 512],
                   start=False, stop=True, inc="t")
            wait("vector", "t")
            op("vector", "reduce_max", stats[:, 16:17], ps_s[:, :], AX.X, negate=True, inc="v")
            acq("scalar", ("ptm", qt % 2))
            wait("scalar", "v")
            op("scalar", "activation", ptm[qt % 2][:, :], ps_s[:, :], AF.Exp,
               bias=stats[:, 16:17], accum_out=rowsums[:, h * 8 + qt:h * 8 + qt + 1], inc="a")
            rel(("pss",), "a")
            if qt > 0:
                norm_and_transpose(qt - 1)
        norm_and_transpose(7)
        if dbg and h == 0:
            wait("sync", "v")
            dma("sync", dbg_t["probs_d"][:, :], ptm[1][:, :], "std")
        acq("tensor", ("pso0",))
        acq("tensor", ("pso1",))
        wait("tensor", "aT")
        for kt in range(16):
            for hf in range(2):
                op("tensor", "matmul", ps_o[hf][:, :], v_sb[j][:, kt * 128:(kt + 1) * 128],
                   pT[:, kt * 1024 + hf * 512:kt * 1024 + (hf + 1) * 512],
                   start=(kt == 0), stop=(kt == 15), inc=("t" if kt == 15 else None))
        rel(("hbuf", j), "t")
        wait("vector", "t")
        for hf in range(2):
            op("vector", "tensor_copy", acc[:, h * 1024 + hf * 512:h * 1024 + (hf + 1) * 512],
               ps_o[hf][:, :], inc="v")
        rel(("pso0",), "v")
        rel(("pso1",), "v")
        pp_step(6)
    if dbg:
        wait("sync", "v")
        dma("sync", dbg_t["acc_d"][:, :], acc[:, :], "std")
    barrier()
    pb_.close()

    pw = ExitStack()
    wo_ch = [pw.enter_context(nc.sbuf_tensor("wo%d" % j, [128, 16 * 512], BF16)) for j in range(2)]
    ao_st = [pw.enter_context(nc.sbuf_tensor("ao%d" % j, [128, 512], F32)) for j in range(2)]
    ps_w = [pw.enter_context(nc.psum_tensor("ps_w%d" % j, [128, 512], F32)) for j in range(2)]
    for oc in range(4):
        j = oc % 2
        sw = "wWo%d" % j
        acq("gpsimd", ("woch", j))
        dma("gpsimd", wo_ch[j].rearrange("p (a b) -> p a b", a=16),
            woT[:, oc * 512:(oc + 1) * 512].rearrange("(a p) o -> p a o", p=128), sw)
        wait("tensor", sw)
        for tt in range(NTQ):
            acq("tensor", ("psw", tt % 2))
            for vj in range(16):
                op("tensor", "matmul", ps_w[tt % 2][:, :],
                   acc[:, vj * 1024 + tt * 128:vj * 1024 + (tt + 1) * 128],
                   wo_ch[j][:, vj * 512:(vj + 1) * 512],
                   start=(vj == 0), stop=(vj == 15), inc=("t" if vj == 15 else None))
            acq("vector", ("aost", tt % 2))
            wait("vector", "t")
            op("vector", "tensor_copy", ao_st[tt % 2][:, :], ps_w[tt % 2][:, :], inc="v")
            rel(("psw", tt % 2), "v")
            sso = "aoS%d" % (tt % 2)
            wait("sync", "v")
            dma("sync", atn[tt * 128:(tt + 1) * 128, oc * 512:(oc + 1) * 512],
                ao_st[tt % 2][:, :], sso)
            rel(("aost", tt % 2), sso)
        rel(("woch", j), "t")
        pp_step(4)
    pp_step(len(pp_tiles))
    barrier()
    pw.close()
    ppstk.close()

    pf0 = ExitStack()
    ps_bc = pf0.enter_context(nc.psum_tensor("ps_bc", [128, 8], F32))
    for wi in range(3):
        op("vector", "reduce_sum", absums[:, wi:wi + 1], pp_tmps[:, wi * 64:(wi + 1) * 64],
           AX.X, inc="v")
        vflush()
    wait("tensor", "v")
    op("tensor", "matmul", ps_bc[:, 0:3], ones[:, :], absums[:, 0:3], start=True, stop=True, inc="t")
    wait("vector", "t")
    for wi in range(3):
        op("vector", "tensor_scalar", wsv[:, 3 + wi:4 + wi], ps_bc[:, wi:wi + 1],
           1.0 / float(D * DFF), 1e-5, ALU.mult, ALU.max, inc="v")
        vflush()
        op("vector", "reciprocal", wsv[:, 9 + wi:10 + wi], wsv[:, 3 + wi:4 + wi], inc="v")
        vflush()
    if dbg:
        wait("sync", "v")
        dma("sync", dbg_t["wsv_d"][:, :], wsv[:, :], "std")
    barrier()
    pf0.close()

    pf1 = ExitStack()
    xqT_sb = pf1.enter_context(nc.sbuf_tensor("xqTs", [128, 16 * 1024], BF16))
    wstg = [pf1.enter_context(nc.sbuf_tensor("wst%d" % j, [128, 1024], F32)) for j in range(2)]
    tt1 = [pf1.enter_context(nc.sbuf_tensor("tt1%d" % j, [128, 1024], F32)) for j in range(2)]
    r32b = [pf1.enter_context(nc.sbuf_tensor("r32b%d" % j, [128, 1024], BF16)) for j in range(2)]
    r64b = pf1.enter_context(nc.sbuf_tensor("r64b", [128, 1024], BF16))
    cmbb = pf1.enter_context(nc.sbuf_tensor("cmbb", [128, 1024], BF16))
    wg_t = [pf1.enter_context(nc.sbuf_tensor("wg_t%d" % j, [128, 16 * 1024], BF16)) for j in range(2)]
    wu_t = [pf1.enter_context(nc.sbuf_tensor("wu_t%d" % j, [128, 16 * 1024], BF16)) for j in range(2)]
    silu_b = pf1.enter_context(nc.sbuf_tensor("silu_b", [128, 1024], F32))
    hpr = [pf1.enter_context(nc.sbuf_tensor("hpr%d" % j, [128, 1024], F32)) for j in range(2)]
    ps_g = [pf1.enter_context(nc.psum_tensor("ps_g%d" % j, [128, 512], F32)) for j in range(4)]
    ps_u = [pf1.enter_context(nc.psum_tensor("ps_u%d" % j, [128, 512], F32)) for j in range(4)]

    dma("sync", xqT_sb.rearrange("p (a t) -> p a t", a=16),
        xqTh.rearrange("(a p) t -> p a t", p=128), "ldf")
    op("vector", "memset", stats[:, 40:48], 0.0, inc="v")
    wait("tensor", "ldf")

    def xqc(dj, a, b):
        return xqT_sb[:, dj * 1024 + a: dj * 1024 + b]

    def tern_tile_f1(c, m):
        par = c % 2
        wi, dj = divmod(m, 16)
        wsrc = (wgT, wuT)[wi]
        wdst = (wg_t, wu_t)[wi][par]
        thc = wi
        jj = m % 2
        if m == 0:
            acq("vector", ("wgt", par))
        sw = "wF%d" % jj
        acq("sync", ("wstg", jj))
        dma("sync", wstg[jj][:, :],
            wsrc[dj * 128:(dj + 1) * 128, c * 1024:(c + 1) * 1024], sw)
        wait("scalar", sw)
        acq("scalar", ("tt1", jj))
        op("scalar", "activation", tt1[jj][:, :], wstg[jj][:, :], AF.Identity,
           scale=wsv[:, 9 + thc:10 + thc], bias=cst[:, 3:4], inc="a")
        rel(("wstg", jj), "a")
        acq("scalar", ("r32b", jj))
        op("scalar", "activation", r32b[jj][:, :], tt1[jj][:, :], AF.Identity,
           bias=cst[:, 4:5], inc="a")
        rel(("tt1", jj), "a")
        wait("vector", "a")
        op("vector", "tensor_scalar", r64b[:, :], r32b[jj][:, :], C64, C64,
           ALU.add, ALU.subtract, inc="v")
        op("vector", "scalar_tensor_tensor", cmbb[:, :], r32b[jj][:, :],
           kmb[:, 0:1], r64b[:, :], ALU.mult, ALU.add, inc="vF")
        rel(("r32b", jj), "vF")
        op("vector", "tensor_scalar", wdst[:, dj * 1024:(dj + 1) * 1024],
           cmbb[:, :], 1.0, -1.0, ALU.min, ALU.max, inc="vF")

    for c in range(8):
        par = c % 2
        if c == 0:
            for m in range(32):
                tern_tile_f1(0, m)
        wait("tensor", "vF", 64 * (c + 1))
        for i in range(NTQ):
            if c + 1 < 8:
                for m4 in range(i * 4, i * 4 + 4):
                    tern_tile_f1(c + 1, m4)
            ii = i % 2
            for hf in range(2):
                acq("tensor", ("psg", ii * 2 + hf))
                for dj in range(16):
                    op("tensor", "matmul", ps_g[ii * 2 + hf][:, :], xqc(dj, i * 128, (i + 1) * 128),
                       wg_t[par][:, dj * 1024 + hf * 512:dj * 1024 + (hf + 1) * 512],
                       start=(dj == 0), stop=(dj == 15), inc=("t" if dj == 15 else None))
                acq("tensor", ("psu", ii * 2 + hf))
                for dj in range(16):
                    op("tensor", "matmul", ps_u[ii * 2 + hf][:, :], xqc(dj, i * 128, (i + 1) * 128),
                       wu_t[par][:, dj * 1024 + hf * 512:dj * 1024 + (hf + 1) * 512],
                       start=(dj == 0), stop=(dj == 15), inc=("t" if dj == 15 else None))
            wait("vector", "t")
            op("vector", "tensor_scalar", stats[:, 16:17], stats[:, 32 + i:33 + i], wsv[:, 3:4],
               None, ALU.mult, inc="v")
            acq("scalar", ("silu",))
            wait("scalar", "v")
            wait("scalar", "t")
            for hf in range(2):
                op("scalar", "activation", silu_b[:, hf * 512:(hf + 1) * 512],
                   ps_g[ii * 2 + hf][:, :], AF.Silu, scale=stats[:, 16:17],
                   inc=("a" if hf == 1 else None))
            rel(("psg", ii * 2), "a")
            rel(("psg", ii * 2 + 1), "a")
            acq("vector", ("hpr", ii))
            wait("vector", "a")
            for hf in range(2):
                op("vector", "tensor_tensor", hpr[ii][:, hf * 512:(hf + 1) * 512],
                   silu_b[:, hf * 512:(hf + 1) * 512], ps_u[ii * 2 + hf][:, :], ALU.mult, inc="v")
            rel(("silu",), "v")
            rel(("psu", ii * 2), "v")
            rel(("psu", ii * 2 + 1), "v")
            op("vector", "reduce_max", stats[:, 17:18], hpr[ii][:, :], AX.X,
               apply_absolute_value=True, inc="v")
            vflush()
            op("vector", "tensor_tensor", stats[:, 40 + i:41 + i], stats[:, 40 + i:41 + i],
               stats[:, 17:18], ALU.max, inc="v")
            sso = "hS%d" % ii
            wait("sync", "v")
            dma("sync", hsp[(i * 8 + c) * 128:(i * 8 + c) * 128 + 128, :], hpr[ii][:, :], sso)
            rel(("hpr", ii), sso)
        rel(("wgt", par), "t")
    barrier()
    pf1.close()

    pf2 = ExitStack()
    wstg2 = [pf2.enter_context(nc.sbuf_tensor("ws2%d" % j, [128, 2048], F32)) for j in range(2)]
    tt2 = [pf2.enter_context(nc.sbuf_tensor("tt2%d" % j, [128, 2048], F32)) for j in range(2)]
    r32c = [pf2.enter_context(nc.sbuf_tensor("r32c%d" % j, [128, 2048], BF16)) for j in range(2)]
    r64c = pf2.enter_context(nc.sbuf_tensor("r64c", [128, 2048], BF16))
    cmbc = pf2.enter_context(nc.sbuf_tensor("cmbc", [128, 2048], BF16))
    wd_t = pf2.enter_context(nc.sbuf_tensor("wd_t", [128, 8 * 2048], BF16))
    hre = [pf2.enter_context(nc.sbuf_tensor("hre%d" % j, [128, 1024], F32)) for j in range(2)]
    t1 = pf2.enter_context(nc.sbuf_tensor("t1F", [128, 1024], F32))
    t2 = pf2.enter_context(nc.sbuf_tensor("t2F", [128, 1024], F32))
    hib = pf2.enter_context(nc.sbuf_tensor("hib", [128, 1024], BF16))
    hiT = pf2.enter_context(nc.sbuf_tensor("hiT", [128, 8 * 128], BF16))
    accd = pf2.enter_context(nc.sbuf_tensor("accd", [128, 8 * 2048], F32))
    xre = [pf2.enter_context(nc.sbuf_tensor("xre%d" % j, [128, 2048], F32)) for j in range(1)]
    are = [pf2.enter_context(nc.sbuf_tensor("are%d" % j, [128, 2048], F32)) for j in range(1)]
    ost = [pf2.enter_context(nc.sbuf_tensor("ost%d" % j, [128, 2048], F32)) for j in range(1)]
    ps_hT = pf2.enter_context(nc.psum_tensor("ps_hT", [128, 1024], BF16))
    ps_d = [pf2.enter_context(nc.psum_tensor("ps_d%d" % j, [128, 512], F32)) for j in range(4)]

    op("vector", "tensor_scalar", stats[:, 48:56], stats[:, 32:40], wsv[:, 4:5], None,
       ALU.mult, inc="v")
    vflush()
    op("vector", "tensor_tensor", stats[:, 40:48], stats[:, 40:48], stats[:, 48:56],
       ALU.mult, inc="v")
    vflush()
    op("vector", "tensor_scalar", stats[:, 40:48], stats[:, 40:48], 1e-5, None, ALU.max, inc="v")
    vflush()
    op("vector", "reciprocal", stats[:, 16:24], stats[:, 40:48], inc="v")
    vflush()
    op("vector", "tensor_scalar", stats[:, 16:24], stats[:, 16:24], 127.0, None, ALU.mult, inc="v")
    vflush()
    op("vector", "tensor_tensor", stats[:, 48:56], stats[:, 48:56], stats[:, 16:24],
       ALU.mult, inc="v")
    vflush()
    op("vector", "tensor_scalar", stats[:, 56:64], stats[:, 40:48], 1.0 / 127.0, None,
       ALU.mult, inc="v")
    vflush()
    op("vector", "tensor_scalar", stats[:, 56:64], stats[:, 56:64], wsv[:, 5:6], None,
       ALU.mult, inc="v")
    vflush()

    def tern_tile_f2(c, fj):
        jj = fj % 2
        if fj == 0:
            acq("vector", ("wdt",))
        sw = "wD%d" % jj
        acq("sync", ("ws2", jj))
        dma("sync", wstg2[jj][:, :], wdT[(c * 8 + fj) * 128:(c * 8 + fj) * 128 + 128, :], sw)
        wait("scalar", sw)
        acq("scalar", ("tt2", jj))
        op("scalar", "activation", tt2[jj][:, :], wstg2[jj][:, :], AF.Identity,
           scale=wsv[:, 11:12], bias=cst[:, 3:4], inc="a")
        rel(("ws2", jj), "a")
        acq("scalar", ("r32c", jj))
        op("scalar", "activation", r32c[jj][:, :], tt2[jj][:, :], AF.Identity,
           bias=cst[:, 4:5], inc="a")
        rel(("tt2", jj), "a")
        wait("vector", "a")
        op("vector", "tensor_scalar", r64c[:, :], r32c[jj][:, :], C64, C64,
           ALU.add, ALU.subtract, inc="v")
        op("vector", "scalar_tensor_tensor", cmbc[:, :], r32c[jj][:, :],
           kmb[:, 0:1], r64c[:, :], ALU.mult, ALU.add, inc="vF2")
        rel(("r32c", jj), "vF2")
        op("vector", "tensor_scalar", wd_t[:, fj * 2048:(fj + 1) * 2048],
           cmbc[:, :], 1.0, -1.0, ALU.min, ALU.max, inc="vF2")

    for c in range(8):
        for fj in range(8):
            tern_tile_f2(c, fj)
        wait("tensor", "vF2", 16 * (c + 1))
        for i in range(NTQ):
            ii = i % 2
            sh = "hr%d" % ii
            acq("sync", ("hre", ii))
            dma("sync", hre[ii][:, :], hsp[(i * 8 + c) * 128:(i * 8 + c) * 128 + 128, :], sh)
            wait("vector", sh)
            op("vector", "tensor_scalar", t1[:, :], hre[ii][:, :], stats[:, 48 + i:49 + i],
               None, ALU.mult, inc="v")
            rel(("hre", ii), "v")
            op("vector", "tensor_scalar", t2[:, :], t1[:, :], C32, C32, ALU.add, ALU.subtract, inc="v")
            op("vector", "tensor_scalar", t1[:, :], t1[:, :], C64, C64, ALU.add, ALU.subtract, inc="v")
            op("vector", "scalar_tensor_tensor", t2[:, :], t2[:, :], kmv[:, 1:2], t1[:, :],
               ALU.mult, ALU.add, inc="v")
            op("vector", "tensor_scalar", hib[:, :], t2[:, :], 127.0, -128.0, ALU.min, ALU.max, inc="v")
            acq("tensor", ("pshT",))
            wait("tensor", "v")
            for q8 in range(8):
                op("tensor", "transpose", ps_hT[:, q8 * 128:(q8 + 1) * 128],
                   hib[:, q8 * 128:(q8 + 1) * 128], ident[:, :], inc="t")
            acq("vector", ("hiT",))
            wait("vector", "t")
            op("vector", "tensor_copy", hiT.rearrange("p (a t) -> p a t", a=8),
               ps_hT.rearrange("p (a t) -> p a t", a=8), inc="v")
            rel(("pshT",), "v")
            wait("tensor", "v")
            for oc in range(4):
                acq("tensor", ("psd", oc))
                for fj in range(8):
                    op("tensor", "matmul", ps_d[oc][:, :], hiT[:, fj * 128:(fj + 1) * 128],
                       wd_t[:, fj * 2048 + oc * 512:fj * 2048 + (oc + 1) * 512],
                       start=(fj == 0), stop=(fj == 7), inc=("t" if fj == 7 else None))
            rel(("hiT",), "t")
            wait("vector", "t")
            for oc in range(4):
                d_ap = accd[:, i * 2048 + oc * 512:i * 2048 + (oc + 1) * 512]
                if c == 0:
                    op("vector", "tensor_copy", d_ap, ps_d[oc][:, :], inc="v")
                else:
                    op("vector", "tensor_tensor", d_ap, d_ap, ps_d[oc][:, :], ALU.add, inc="v")
                rel(("psd", oc), "v")
        rel(("wdt",), "t")
    for i in range(NTQ):
        ii = 0
        so = "xo%d" % ii
        acq("sync", ("xre", ii))
        dma("sync", xre[ii][:, :], x_h[i * 128:(i + 1) * 128, :], so)
        dma("sync", are[ii][:, :], atn[i * 128:(i + 1) * 128, :], so)
        wait("vector", so)
        acq("vector", ("ost", ii))
        op("vector", "scalar_tensor_tensor", ost[ii][:, :],
           accd[:, i * 2048:(i + 1) * 2048], stats[:, 56 + i:57 + i],
           xre[ii][:, :], ALU.mult, ALU.add, inc="v")
        op("vector", "tensor_tensor", ost[ii][:, :], ost[ii][:, :], are[ii][:, :], ALU.add, inc="v")
        rel(("xre", ii), "v")
        sso = "sto%d" % ii
        wait("sync", "v")
        dma("sync", out_h[i * 128:(i + 1) * 128, :], ost[ii][:, :], sso)
        rel(("ost", ii), sso)
    wait("sync", "sto0", cnt["sto0"])
    if dbg:
        wait("sync", "std", cnt.get("std", 0))
    pf2.close()

    with nc.Block() as block:
        def runner(fns):
            def body(e):
                for f in fns:
                    f(e)
            return body
        block.sync(runner(streams["sync"]))
        block.scalar(runner(streams["scalar"]))
        block.vector(runner(streams["vector"]))
        block.tensor(runner(streams["tensor"]))
        block.gpsimd(runner(streams["gpsimd"]))
    st.close()
    return nc, dbg_t


def host_inputs(L0_attn_hidden_states, L0_attn_Wq, L0_attn_Wk, L0_attn_Wv, L0_attn_Wo,
                L0_ffn_w_gate, L0_ffn_w_up, L0_ffn_w_down):
    x = np.ascontiguousarray(np.asarray(L0_attn_hidden_states, dtype=np.float32))
    wqT_ = np.ascontiguousarray(np.asarray(L0_attn_Wq, np.float32).T)
    wkT_ = np.ascontiguousarray(np.asarray(L0_attn_Wk, np.float32).T)
    wvT_ = np.ascontiguousarray(np.asarray(L0_attn_Wv, np.float32).T)
    woT_ = np.ascontiguousarray(np.asarray(L0_attn_Wo, np.float32).T)
    wgT_ = np.ascontiguousarray(np.asarray(L0_ffn_w_gate, np.float32).T)
    wuT_ = np.ascontiguousarray(np.asarray(L0_ffn_w_up, np.float32).T)
    wdT_ = np.ascontiguousarray(np.asarray(L0_ffn_w_down, np.float32).T)
    inv_freq = (1.0 / (10000.0 ** (np.arange(0, DR, 2, dtype=np.float32) / np.float32(DR)))).astype(np.float32)
    t = np.arange(S, dtype=np.float32)
    freqs = (t[:, None] * inv_freq[None, :]).astype(np.float32)
    cos = np.cos(freqs).astype(np.float32)
    sin = np.sin(freqs).astype(np.float32)
    ctabH = np.ascontiguousarray(np.tile(cos, (1, H)))
    stabH = np.ascontiguousarray(np.tile(sin, (1, H)))
    identm = np.eye(128, dtype=ml_dtypes.bfloat16)
    onesm = np.ones((128, 128), np.float32)
    cstm = np.zeros((128, 8), np.float32)
    cstm[:, 0] = 1.0
    cstm[:, 1] = 1e-5
    cstm[:, 2] = 1.0 / np.float32(D)
    cstm[:, 3] = np.float32(C32)
    cstm[:, 4] = np.float32(-C32)
    in_maps = []
    for c in range(8):
        b, hh = c // 2, c % 2
        xb = np.ascontiguousarray(np.roll(x[b], -1024 * hh, axis=0))
        ct = np.ascontiguousarray(np.roll(ctabH, -1024 * hh, axis=0))
        sta = np.ascontiguousarray(np.roll(stabH, -1024 * hh, axis=0))
        in_maps.append(dict(x=xb, wqT=wqT_, wkT=wkT_, wvT=wvT_, woT=woT_, wgT=wgT_, wuT=wuT_,
                            wdT=wdT_, ctab=ct, stab=sta, ident=identm, ones=onesm, cst=cstm))
    return in_maps


def kernel(**inputs):
    in_maps = host_inputs(**inputs)
    nc, _ = build(dbg=False)
    res = run_bass_kernel_spmd(nc, in_maps, core_ids=list(range(8)))
    out = np.empty((4, S, D), np.float32)
    for c in range(8):
        b, hh = c // 2, c % 2
        out[b, hh * 1024:(hh + 1) * 1024, :] = res.results[c]["out"]
    return out
